# version 03 (final)
# baseline (speedup 1.0000x reference)
import numpy as np
import ml_dtypes
from contextlib import ExitStack

import concourse.bass as bass
import concourse.mybir as mybir
from concourse.bass_utils import run_bass_kernel_spmd

F32 = mybir.dt.float32
BF16 = mybir.dt.bfloat16
I32 = mybir.dt.int32
AX = mybir.AxisListType.X
ALU = mybir.AluOpType
ACTF = mybir.ActivationFunctionType

D = 4096
NH = 32
HD = 128
NB = 2
NCORES = 8
TLOC = 1024
NT = 8
ICN = 32
RSC = 0.08838834764831845
SEGS = [(0, 7.0), (1024, 7.0), (2048, 31.0), (3072, 127.0)]
LN_EPS = 1e-5
RMS_EPS = 1.1920928955078125e-07
SINKHORN_ITERS = 50
NEG = -1.0e30


class MK:
    ENGS = ("sync", "vector", "scalar", "tensor", "gpsimd")

    def __init__(self):
        self.streams = {e: [] for e in self.ENGS}
        self.cnt = {}
        self.sems = {}

    def emit(self, eng, make, waits=(), inc=None, step=1, drain=False):
        if inc is not None:
            self.cnt[inc] += step
        sems = self.sems

        def fn(e, make=make, waits=tuple(waits), inc=inc, step=step, drain=drain):
            for sn, v in waits:
                e.wait_ge(sems[sn], v)
            if drain:
                e.drain()
            r = make(e)
            if inc is not None and r is not None:
                r.then_inc(sems[inc], step)
        self.streams[eng].append(fn)

    def wait(self, eng, waits):
        sems = self.sems

        def fn(e, waits=tuple(waits)):
            for sn, v in waits:
                e.wait_ge(sems[sn], v)
        self.streams[eng].append(fn)

    def now(self, sem):
        return self.cnt[sem]

    def dma(self, queue, out, in_, sem, waits=()):
        waits = list(waits) + [(sem, self.cnt[sem])]
        self.emit(queue, lambda e, o=out, i=in_: e.dma_start(out=o, in_=i),
                  waits=waits, inc=sem, step=16)
        return self.cnt[sem]

    def barrier(self):
        self.emit("vector", self.bar_ops["vector"], inc="V")
        self.emit("scalar", self.bar_ops["scalar"], inc="S")
        self.emit("tensor", self.bar_ops["tensor"],
                  waits=[("V", self.cnt["V"]), ("S", self.cnt["S"])], inc="T")
        self.emit("gpsimd", self.bar_ops["gpsimd"], inc="G")
        targets = tuple((sn, self.cnt[sn]) for sn in ("V", "S", "T", "G", "LD", "GP"))
        for eng in self.ENGS:
            self.wait(eng, targets)


def build(upto=99, dbg=()):
    mk = MK()
    nc = bass.Bass("TRN2", target_bir_lowering=False)

    def dram(name, shape, dt, kind=None):
        k = kind
        if k is None:
            k = "ExternalOutput" if name in dbg else None
        if k is None:
            return nc.dram_tensor(name, shape, dt).ap()
        return nc.dram_tensor(name, shape, dt, kind=k).ap()

    xin = dram("xin", [TLOC, D], F32, "ExternalInput")
    w4 = dram("w4", [4, 512, D], F32, "ExternalInput")
    cs = dram("cs", [512, 128], F32, "ExternalInput")
    maskin = dram("maskin", [128, 1024], BF16, "ExternalInput")
    identin = dram("identin", [128, 128], BF16, "ExternalInput")
    wsm = dram("wsm", [D, 24], F32, "ExternalInput")
    hbase = dram("hbase", [1, 25], F32, "ExternalInput")
    wbr = dram("wbr", [1024, 1024], F32, "ExternalInput")
    wqkv = dram("wqkv", [D, 6144], F32, "ExternalInput")
    wproj = dram("wproj", [D, D], F32, "ExternalInput")
    wbase = dram("wbase", [D, D], F32, "ExternalInput")
    bvec = dram("bvec", [1, D], F32, "ExternalInput")
    wa = dram("wa", [D, 80], F32, "ExternalInput")
    bmat = dram("bmat", [1024, D], F32, "ExternalInput")
    out = dram("out", [TLOC, D], F32, "ExternalOutput")

    wq_c = dram("wq_c", [4, D, 512], BF16)
    wq_g = dram("wq_g", [32, D, 512], BF16)
    kv_c = dram("kv_c", [2, NB * NH * HD * 512], BF16)
    kv_g = dram("kv_g", [16, NB * NH * HD * 512], BF16)
    qloc = dram("qloc", [NB, NH, HD, 512], BF16)
    qkvb = dram("qkvb", [NT, 128, 6144], BF16)
    stg = dram("stg", [NT, 128, 1024], BF16)
    dbgout = dram("dbgout", [NT, 128, D], F32) if ("dbgout" in dbg) else None

    es = ExitStack()
    sbt = lambda n, s, d: es.enter_context(nc.sbuf_tensor(n, s, d))
    pst = lambda n, s, d: es.enter_context(nc.psum_tensor(n, s, d))
    with es:
        xres = sbt("xres", [128, NT * D], BF16)
        arena = sbt("arena", [128, NT * D], BF16)
        S1 = sbt("S1", [128, D], F32)
        S2 = sbt("S2", [128, D], F32)
        SB1 = sbt("SB1", [128, D], BF16)
        SB2 = sbt("SB2", [128, 2048], BF16)
        wrhs = sbt("wrhs", [128, D], BF16)
        qb = sbt("qb", [128, 512], BF16)
        small = sbt("small", [128, 64], F32)
        sm2 = sbt("sm2", [128, 128], F32)
        skA = sbt("skA", [128, 128], F32)
        skB = sbt("skB", [128, 256], F32)
        smallb = sbt("smallb", [128, 896], BF16)
        ident = sbt("ident", [128, 128], BF16)
        cst_cm = nc.sbuf_tensor("cst", [128, 512], F32)
        cst = cst_cm.__enter__()
        cst_freed = [False]

        pbA = pst("pbA", [128, 2048], F32)
        pbB = pst("pbB", [128, 2048], F32)

        for sn in ("LD", "GP", "CC", "V", "S", "T", "G"):
            mk.sems[sn] = es.enter_context(nc.semaphore(sn))
            mk.cnt[sn] = 0
        block = es.enter_context(nc.Block())
        mk.bar_ops = {
            "vector": lambda e: e.memset(small[0:1, 62:63], 0.0),
            "scalar": lambda e: e.activation(small[0:1, 59:60], ident[0:1, 0:1], ACTF.Copy),
            "tensor": lambda e: e.transpose(
                pbA[:, 0:512].bitcast(BF16)[:, 0:128], ident[:, :], ident[:, :]),
            "gpsimd": lambda e: e.memset(small[0:1, 61:62], 0.0),
        }

        E, W, now, dma, barrier = mk.emit, mk.wait, mk.now, mk.dma, mk.barrier

        def bank(t):
            return (pbA if t < 4 else pbB)[:, 512 * (t % 4): 512 * (t % 4 + 1)]

        def quant_tile(src, dst, waits0=()):
            first = True
            for (c0, maxq) in SEGS:
                K = float(maxq + 1.0)
                gsl = src[:, c0:c0 + 1024]
                E("vector", lambda e, g=gsl: e.tensor_reduce(
                    small[:, 0:32], g.rearrange("p (g k) -> p g k", k=32), AX, ALU.max,
                    apply_absolute_value=True),
                    waits=(waits0 if first else ()), drain=True)
                first = False
                E("vector", lambda e: e.tensor_scalar(
                    sm2[:, 0:32], small[:, 0:32], 1e-37, None, ALU.max), drain=True)
                E("vector", lambda e: e.reciprocal(
                    sm2[:, 64:96], sm2[:, 0:32]), drain=True)
                E("vector", lambda e, mq=maxq: e.tensor_scalar(
                    sm2[:, 32:64], small[:, 0:32], 1.0 / mq, None, ALU.mult), drain=True)
                E("vector", lambda e, g=gsl: e.tensor_tensor(
                    S2[:, 0:1024].rearrange("p (g k) -> p g k", k=32),
                    g.rearrange("p (g k) -> p g k", k=32),
                    sm2[:, 64:96].unsqueeze(2).broadcast_to((128, 32, 32)),
                    ALU.mult), drain=True, inc="V")
                ti = S2[:, 1024:2048].bitcast(I32)
                E("scalar", lambda e, mq=maxq, K=K, ti=ti: e.activation(
                    ti, S2[:, 0:1024], ACTF.Copy, bias=K, scale=float(mq)),
                    waits=[("V", now("V"))], inc="S")
                E("vector", lambda e, ti=ti, mq=maxq: e.scalar_tensor_tensor(
                    S2[:, 2048:3072], S2[:, 0:1024], float(mq), ti, ALU.mult, ALU.subtract),
                    waits=[("S", now("S"))], drain=True)
                E("vector", lambda e, ti=ti, th=0.5 - K: e.scalar_tensor_tensor(
                    S2[:, 0:1024], S2[:, 2048:3072], th, ti, ALU.is_gt, ALU.add), drain=True)
                E("vector", lambda e, K=K, d=dst, c0=c0: e.scalar_tensor_tensor(
                    d[:, c0:c0 + 1024].rearrange("p (g k) -> p g k", k=32),
                    S2[:, 0:1024].rearrange("p (g k) -> p g k", k=32), -K,
                    sm2[:, 32:64].unsqueeze(2).broadcast_to((128, 32, 32)),
                    ALU.add, ALU.mult), drain=True, inc="V")

        def transpose_region(src, n, dst_fn, vwait):
            for g0 in range(0, n, 4):
                gn = min(4, n - g0)
                for j in range(gn):
                    i = g0 + j
                    E("tensor", lambda e, i=i, j=j: e.transpose(
                        pbA[:, 0:512].bitcast(BF16)[:, 128 * j:128 * (j + 1)],
                        src[:, 128 * i:128 * (i + 1)], ident[:, :]),
                        waits=([("V", vwait)] if (g0 == 0 and j == 0) else ()), inc="T")
                tt = now("T")
                for j in range(gn):
                    i = g0 + j
                    E("vector", lambda e, i=i, j=j: e.tensor_copy(
                        dst_fn(i), pbA[:, 0:512].bitcast(BF16)[:, 128 * j:128 * (j + 1)]),
                        waits=([("T", tt)] if j == 0 else ()), drain=(j == 0), inc="V")
                W("tensor", [("V", now("V"))])

        def ln_from(r_ap, sq_ap, dst_ap, vr):
            E("scalar", lambda e: e.activation(
                sq_ap[:, 0:2048], r_ap[:, 0:2048], ACTF.Square, accum_out=small[:, 40:41]),
                waits=[("V", vr)], inc="S")
            E("scalar", lambda e: e.activation(
                sq_ap[:, 2048:4096], r_ap[:, 2048:4096], ACTF.Square, accum_out=small[:, 41:42]),
                inc="S")
            E("vector", lambda e: e.tensor_reduce(
                small[:, 42:43], r_ap[:, 0:4096], AX, ALU.add), drain=True)
            E("vector", lambda e: e.tensor_scalar(
                small[:, 43:44], small[:, 42:43], 1.0 / D, None, ALU.mult),
                waits=[("S", now("S"))], drain=True)
            E("vector", lambda e: e.tensor_tensor(
                small[:, 44:45], small[:, 40:41], small[:, 41:42], ALU.add), drain=True)
            E("vector", lambda e: e.tensor_tensor(
                small[:, 45:46], small[:, 43:44], small[:, 43:44], ALU.mult), drain=True)
            E("vector", lambda e: e.tensor_scalar(
                small[:, 51:52], small[:, 45:46], LN_EPS, None, ALU.subtract), drain=True)
            E("vector", lambda e: e.scalar_tensor_tensor(
                small[:, 46:47], small[:, 44:45], 1.0 / D, small[:, 51:52],
                ALU.mult, ALU.subtract), drain=True, inc="V")
            E("scalar", lambda e: e.activation(
                small[:, 47:48], small[:, 46:47], ACTF.Sqrt),
                waits=[("V", now("V"))], inc="S")
            E("vector", lambda e: e.reciprocal(small[:, 48:49], small[:, 47:48]),
              waits=[("S", now("S"))], drain=True)
            E("vector", lambda e: e.tensor_tensor(
                small[:, 49:50], small[:, 43:44], small[:, 48:49], ALU.mult), drain=True)
            E("vector", lambda e: e.tensor_scalar(
                small[:, 50:51], small[:, 49:50], -1.0, None, ALU.mult), drain=True)
            if dst_ap is not None:
                E("vector", lambda e: e.tensor_scalar(
                    dst_ap, r_ap[:, 0:4096], small[:, 48:49], small[:, 50:51],
                    ALU.mult, ALU.add), drain=True, inc="V")

        pend = {"buf_T": [0, 0]}

        def proj_ocouter(rhs_src, lhsT_chunk, epilogue, oc_count=8, ic_count=32,
                         rhs_cast=True, extra_mms=None, wait_start=()):
            ngrp = ic_count // 4
            first = True
            for oc in range(oc_count):
                for icg in range(ngrp):
                    buf = (oc * ngrp + icg) % 2
                    v = dma("gpsimd" if rhs_cast else "sync",
                            wrhs[:, buf * 2048: buf * 2048 + 2048].rearrange(
                                "p (i o) -> p i o", o=512),
                            rhs_src(oc, icg), "GP" if rhs_cast else "LD",
                            waits=[("T", pend["buf_T"][buf])])
                    for i4 in range(4):
                        ic = icg * 4 + i4
                        last = (ic == ic_count - 1) and (extra_mms is None)
                        for t in range(NT):
                            wl = []
                            if i4 == 0 and t == 0:
                                wl.append(("GP" if rhs_cast else "LD", v))
                                if first:
                                    wl.extend(wait_start)
                                    first = False
                            E("tensor", lambda e, ic=ic, t=t, buf=buf, i4=i4, last=last: e.matmul(
                                bank(t), lhsT_chunk(ic, t),
                                wrhs[:, buf * 2048 + i4 * 512: buf * 2048 + (i4 + 1) * 512],
                                start=(ic == 0), stop=last),
                                waits=wl,
                                inc=("T" if (i4 == 3 and t == NT - 1) else None))
                    pend["buf_T"][buf] = now("T")
                if extra_mms is not None:
                    extra_mms(oc)
                tval = now("T")
                for t in range(NT):
                    epilogue(oc, t, tval)
                W("tensor", [("V", now("V")), ("S", now("S"))])

        dma("sync", cst[:, :].rearrange("p (a j) -> p a j", j=128),
            cs.rearrange("(a p) j -> p a j", p=128), "LD")
        dma("sync", ident[:, :], identin, "LD")
        dma("sync", skB[0:1, 192:217], hbase, "LD")
        ld0 = now("LD")
        W("vector", [("LD", ld0)])
        W("tensor", [("LD", ld0)])
        W("scalar", [("LD", ld0)])

        if upto >= 1:
            ldprev = [0, 0]
            for m in range(4):
                for r in range(4):
                    ld = dma("sync", S1[:, 0:D], w4[m, 128 * r:128 * (r + 1), :], "LD",
                             waits=[("V", now("V"))])
                    quant_tile(S1[:, 0:D], arena[:, r * D:(r + 1) * D],
                               waits0=[("LD", ld)])
                vq = now("V")
                for ic in range(ICN):
                    sl = ic % 2
                    for r in range(4):
                        E("tensor", lambda e, ic=ic, r=r: e.transpose(
                            pbA[:, 0:512].bitcast(BF16)[:, 128 * r:128 * (r + 1)],
                            arena[:, r * D + 128 * ic: r * D + 128 * (ic + 1)], ident[:, :]),
                            waits=([("V", vq)] if (ic == 0 and r == 0) else ()), inc="T")
                    tt = now("T")
                    E("vector", lambda e, sl=sl: e.tensor_copy(
                        SB2[:, 512 * sl:512 * (sl + 1)],
                        pbA[:, 0:512].bitcast(BF16)[:, 0:512]),
                        waits=[("T", tt), ("LD", ldprev[sl])], drain=True, inc="V")
                    ldprev[sl] = dma(
                        "sync", wq_c[m, 128 * ic:128 * (ic + 1), :],
                        SB2[:, 512 * sl:512 * (sl + 1)], "LD", waits=[("V", now("V"))])
                    W("tensor", [("V", now("V"))])
            E("gpsimd", lambda e: e.collective_compute(
                "AllGather", ALU.bypass, replica_groups=[list(range(NCORES))],
                ins=[wq_c.opt()], outs=[wq_g.opt()]),
                waits=[("LD", now("LD"))], inc="CC")

        if upto >= 2:
            for t in range(NT):
                ld = dma("sync", S1[:, 0:D], xin[128 * t:128 * (t + 1), :], "LD",
                         waits=[("V", now("V"))])
                E("vector", lambda e, t=t: e.tensor_copy(
                    xres[:, t * D:(t + 1) * D], S1[:, 0:D]),
                    waits=[("LD", ld)], inc="V")
                quant_tile(S1[:, 0:D], SB1)
                transpose_region(
                    SB1, ICN,
                    (lambda t: (lambda ic: arena[:, ic * 1024 + t * 128: ic * 1024 + (t + 1) * 128]))(t),
                    now("V"))

        if upto >= 3:
            W("tensor", [("CC", 1)])
            W("sync", [("CC", 1)])

            def w4_rhs(m):
                def f(oc, icg):
                    return wq_g[4 * oc + m, 512 * icg:512 * (icg + 1), :].rearrange(
                        "(i p) o -> p i o", p=128)
                return f

            def xqT_chunk(ic, t):
                return arena[:, ic * 1024 + t * 128: ic * 1024 + (t + 1) * 128]

            def qkv_epilogue(m):
                def epi(oc, t, tval):
                    qt = t % 4
                    b = t // 4
                    ps = bank(t)
                    if m == 2:
                        E("vector", lambda e, ps=ps: e.tensor_copy(SB2[:, 0:512], ps),
                          waits=[("T", tval), ("LD", now("LD"))], inc="V", drain=True)
                        dst = kv_c[1:2, :].rearrange(
                            "one (bb tt ff) -> one bb tt ff", bb=2, tt=512)[
                                0, b, 128 * qt:128 * (qt + 1), 512 * oc:512 * (oc + 1)]
                        dma("sync", dst, SB2[:, 0:512], "LD", waits=[("V", now("V"))])
                    else:
                        csl = cst[:, 128 * qt:128 * qt + 64].unsqueeze(1).broadcast_to((128, 4, 64))
                        ssl = cst[:, 128 * qt + 64:128 * qt + 128].unsqueeze(1).broadcast_to((128, 4, 64))
                        pv = ps.rearrange("p (h d) -> p h d", d=128)
                        q1, q2 = pv[:, :, 0:64], pv[:, :, 64:128]
                        sc = RSC if m == 0 else 1.0
                        ov = SB1[:, 0:512].rearrange("p (h d) -> p h d", d=128)
                        s2a = S2[:, 0:256].rearrange("p (h d) -> p h d", d=64)
                        s2b = S2[:, 256:512].rearrange("p (h d) -> p h d", d=64)
                        E("vector", lambda e, q1=q1, csl=csl, sc=sc, s2a=s2a: e.scalar_tensor_tensor(
                            s2a, q1, sc, csl, ALU.mult, ALU.mult),
                            waits=[("T", tval)], drain=True)
                        E("vector", lambda e, q2=q2, ssl=ssl, sc=sc, s2b=s2b: e.scalar_tensor_tensor(
                            s2b, q2, sc, ssl, ALU.mult, ALU.mult))
                        E("vector", lambda e, ov=ov, s2a=s2a, s2b=s2b: e.tensor_tensor(
                            ov[:, :, 0:64], s2a, s2b, ALU.subtract), drain=True)
                        E("vector", lambda e, q2=q2, csl=csl, sc=sc, s2a=s2a: e.scalar_tensor_tensor(
                            s2a, q2, sc, csl, ALU.mult, ALU.mult), drain=True)
                        E("vector", lambda e, q1=q1, ssl=ssl, sc=sc, s2b=s2b: e.scalar_tensor_tensor(
                            s2b, q1, sc, ssl, ALU.mult, ALU.mult))
                        E("vector", lambda e, ov=ov, s2a=s2a, s2b=s2b: e.tensor_tensor(
                            ov[:, :, 64:128], s2a, s2b, ALU.add), drain=True, inc="V")
                        vv = now("V")
                        for hh in range(4):
                            E("tensor", lambda e, hh=hh, ps=ps: e.transpose(
                                ps.bitcast(BF16)[:, 128 * hh:128 * (hh + 1)],
                                SB1[:, 128 * hh:128 * (hh + 1)], ident[:, :]),
                                waits=([("V", vv)] if hh == 0 else ()), inc="T")
                        ttv = now("T")
                        E("vector", lambda e, ps=ps: e.tensor_copy(
                            SB2[:, 0:512], ps.bitcast(BF16)[:, 0:512]),
                            waits=[("T", ttv), ("LD", now("LD"))], drain=True, inc="V")
                        if m == 0:
                            dst = qloc[b, 4 * oc:4 * oc + 4, :, 128 * qt:128 * (qt + 1)]
                        else:
                            dst = kv_c[0:1, :].rearrange(
                                "one (bb hh dd tt) -> one bb hh dd tt",
                                bb=2, hh=32, dd=128)[0, b, 4 * oc:4 * oc + 4, :, 128 * qt:128 * (qt + 1)]
                        dst = dst.rearrange("h d t -> d h t")
                        dma("sync", dst, SB2[:, 0:512].rearrange("p (h t) -> p h t", t=128), "LD",
                            waits=[("V", now("V"))])
                        W("tensor", [("V", now("V"))])
                return epi

            for m in range(3):
                proj_ocouter(w4_rhs(m), xqT_chunk, qkv_epilogue(m), rhs_cast=False)

            E("gpsimd", lambda e: e.collective_compute(
                "AllGather", ALU.bypass, replica_groups=[list(range(NCORES))],
                ins=[kv_c.opt()], outs=[kv_g.opt()]),
                waits=[("LD", now("LD"))], inc="CC")
            if "kv_dbg" in dbg:
                kvdbg = dram("kv_dbg", [16, NB * NH * HD * 512], BF16, "ExternalOutput")
                W("sync", [("CC", 2)])
                for rr in range(16):
                    dma("sync", kvdbg[rr], kv_g[rr], "LD")

        if upto >= 4:
            for eng in ("sync", "vector", "scalar", "tensor", "gpsimd"):
                W(eng, [("CC", 2)])
            dma("sync", wrhs[:, 0:1024], maskin, "LD", waits=[("T", now("T"))])
            ldmask = now("LD")
            W("tensor", [("LD", ldmask)])

            kwin = lambda buf: SB1[:, 1024 * buf:1024 * (buf + 1)]
            vwin = lambda buf: SB2[:, 1024 * buf:1024 * (buf + 1)]
            ptile = lambda pp: SB1[:, 2048 + 512 * pp: 2560 + 512 * pp]
            ptT = lambda pp: SB1[:, 3072 + 512 * pp: 3584 + 512 * pp]
            st = {"sfree": [0, 0], "pbB": [0, 0], "av": [0, 0], "tuse": [0, 0],
                  "sc": 0, "tlast": now("T")}
            kview = [kv_g[2 * C:2 * C + 1, :].rearrange(
                "one (bb hh dd tt) -> one bb hh dd tt", bb=2, hh=32, dd=128)
                for C in range(NCORES)]
            vview = [kv_g[2 * C + 1:2 * C + 2, :].rearrange(
                "one (bb tt ff) -> one bb tt ff", bb=2, tt=512)
                for C in range(NCORES)]

            for b in range(NB):
                for h in range(NH):
                    ldq = dma("sync", qb[:, :], qloc[b, h, :, :], "LD",
                              waits=[("T", st["tlast"])])
                    E("vector", lambda e: e.memset(small[:, 0:1], -3.0e38),
                      waits=[("S", now("S"))], drain=True)
                    for j in range(1, 4):
                        E("vector", lambda e, j=j: e.memset(small[:, 8 * j:8 * j + 1], -3.0e38))
                    for j in range(4):
                        E("vector", lambda e, j=j: e.memset(small[:, 8 * j + 1:8 * j + 2], 0.0))
                        E("vector", lambda e, j=j: e.memset(S2[:, 128 * j:128 * (j + 1)], 0.0))
                    E("vector", lambda e: e.memset(small[:, 7:8], 0.0), inc="V")
                    par = [0, 0, 0, 0]
                    lp = [0, 0, 0, 0]
                    for J in range(4):
                        buf = J % 2
                        q_, sem_ = (("sync", "LD") if buf == 0 else ("gpsimd", "GP"))
                        for C in range(NCORES):
                            dma(q_, kwin(buf)[:, 128 * C:128 * (C + 1)],
                                kview[C][0, b, h, :, 128 * J:128 * (J + 1)], sem_,
                                waits=([("T", st["tuse"][buf])] if C == 0 else ()))
                        for C in range(NCORES):
                            dma(q_, vwin(buf)[:, 128 * C:128 * (C + 1)],
                                vview[C][0, b, 128 * J:128 * (J + 1), 128 * h:128 * (h + 1)],
                                sem_)
                        ldw = now(sem_)
                        for j in range(J, 4):
                            for s in range(2):
                                pp = st["sc"] % 2
                                st["sc"] += 1
                                diag = (J == j)
                                wl = [(sem_, ldw), ("S", st["sfree"][pp]),
                                      ("LD", ldq)]
                                E("tensor", lambda e, j=j, s=s, pp=pp, buf=buf, diag=diag: e.matmul(
                                    pbA[:, 512 * pp:512 * (pp + 1)], qb[:, 128 * j:128 * (j + 1)],
                                    kwin(buf)[:, 512 * s:512 * (s + 1)],
                                    start=True, stop=(not diag)),
                                    waits=wl, inc=(None if diag else "T"))
                                if diag:
                                    E("tensor", lambda e, s=s, pp=pp: e.matmul(
                                        pbA[:, 512 * pp:512 * (pp + 1)], ident[:, :],
                                        wrhs[:, 512 * s:512 * (s + 1)],
                                        start=False, stop=True), inc="T")
                                tS = now("T")
                                E("vector", lambda e, j=j, pp=pp: e.tensor_reduce(
                                    small[:, 8 * j + 2:8 * j + 3],
                                    pbA[:, 512 * pp:512 * (pp + 1)], AX, ALU.max),
                                    waits=[("T", tS)], drain=True)
                                E("vector", lambda e, j=j: e.tensor_tensor(
                                    small[:, 8 * j + 3:8 * j + 4], small[:, 8 * j:8 * j + 1],
                                    small[:, 8 * j + 2:8 * j + 3], ALU.max), drain=True)
                                E("vector", lambda e, j=j: e.tensor_scalar(
                                    small[:, 8 * j + 4:8 * j + 5], small[:, 8 * j + 3:8 * j + 4],
                                    -1.0, None, ALU.mult), drain=True, inc="V")
                                vstat = now("V")
                                E("scalar", lambda e, j=j: e.activation(
                                    small[:, 8 * j + 5:8 * j + 6], small[:, 8 * j:8 * j + 1],
                                    ACTF.Exp, bias=small[:, 8 * j + 4:8 * j + 5], scale=1.0),
                                    waits=[("V", vstat)], inc="S")
                                E("scalar", lambda e, j=j, pp=pp: e.activation(
                                    ptile(pp), pbA[:, 512 * pp:512 * (pp + 1)], ACTF.Exp,
                                    bias=small[:, 8 * j + 4:8 * j + 5], scale=1.0,
                                    accum_out=small[:, 8 * j + 6:8 * j + 7]), inc="S")
                                sE = now("S")
                                st["sfree"][pp] = sE
                                lsrc, ldst = 8 * j + 1 + 6 * lp[j], 8 * j + 1 + 6 * (1 - lp[j])
                                lp[j] = 1 - lp[j]
                                E("vector", lambda e, j=j, lsrc=lsrc, ldst=ldst: e.scalar_tensor_tensor(
                                    small[:, ldst:ldst + 1], small[:, lsrc:lsrc + 1],
                                    small[:, 8 * j + 5:8 * j + 6],
                                    small[:, 8 * j + 6:8 * j + 7], ALU.mult, ALU.add),
                                    waits=[("S", sE)], drain=True)
                                E("vector", lambda e, j=j: e.tensor_copy(
                                    small[:, 8 * j:8 * j + 1], small[:, 8 * j + 3:8 * j + 4]),
                                    drain=True, inc="V")
                                for kk in range(4):
                                    E("tensor", lambda e, kk=kk, pp=pp: e.transpose(
                                        pbB[:, 512 * pp:512 * (pp + 1)].bitcast(BF16)[:, 128 * kk:128 * (kk + 1)],
                                        ptile(pp)[:, 128 * kk:128 * (kk + 1)], ident[:, :]),
                                        waits=([("S", sE), ("V", st["pbB"][pp])] if kk == 0 else ()),
                                        inc="T")
                                tPT = now("T")
                                E("vector", lambda e, pp=pp: e.tensor_copy(
                                    ptT(pp), pbB[:, 512 * pp:512 * (pp + 1)].bitcast(BF16)[:, 0:512]),
                                    waits=[("T", tPT)], drain=True, inc="V")
                                vPT = now("V")
                                st["pbB"][pp] = vPT
                                for kk in range(4):
                                    E("tensor", lambda e, kk=kk, pp=pp, buf=buf, s=s: e.matmul(
                                        pbA[:, 1024 + 128 * pp:1152 + 128 * pp],
                                        ptT(pp)[:, 128 * kk:128 * (kk + 1)],
                                        vwin(buf)[:, 512 * s + 128 * kk:512 * s + 128 * (kk + 1)],
                                        start=(kk == 0), stop=(kk == 3)),
                                        waits=([("V", vPT), ("V", st["av"][pp])] if kk == 0 else ()),
                                        inc=("T" if kk == 3 else None))
                                tAV = now("T")
                                osrc = S2[:, 1024 * par[j] + 128 * j:1024 * par[j] + 128 * (j + 1)]
                                odst = S2[:, 1024 * (1 - par[j]) + 128 * j:1024 * (1 - par[j]) + 128 * (j + 1)]
                                par[j] = 1 - par[j]
                                E("vector", lambda e, j=j, pp=pp, osrc=osrc, odst=odst: e.scalar_tensor_tensor(
                                    odst, osrc, small[:, 8 * j + 5:8 * j + 6],
                                    pbA[:, 1024 + 128 * pp:1152 + 128 * pp], ALU.mult, ALU.add),
                                    waits=[("T", tAV)], drain=True, inc="V")
                                st["av"][pp] = now("V")
                        st["tuse"][buf] = now("T")
                    st["tlast"] = now("T")
                    for j in range(4):
                        t_tile = 4 * b + j
                        E("vector", lambda e, j=j: e.reciprocal(
                            small[:, 8 * j + 2:8 * j + 3], small[:, 8 * j + 1:8 * j + 2]),
                            drain=True)
                        E("vector", lambda e, j=j, t_tile=t_tile, h=h: e.tensor_scalar(
                            arena[:, t_tile * D + 128 * h:t_tile * D + 128 * (h + 1)],
                            S2[:, 128 * j:128 * (j + 1)], small[:, 8 * j + 2:8 * j + 3],
                            None, ALU.mult), drain=True, inc="V")

        def proj_ln_epilogue(oc, t, tval):
            E("vector", lambda e, t=t: e.tensor_copy(SB2[:, 0:512], bank(t)),
              waits=[("T", tval), ("LD", now("LD"))], drain=True, inc="V")
            dma("sync", qkvb[t, :, 512 * oc:512 * (oc + 1)], SB2[:, 0:512], "LD",
                waits=[("V", now("V"))])

        def reload_res_ln(dst_fn):
            for t in range(NT):
                ld = dma("sync", SB1[:, 0:D], qkvb[t, :, 0:D], "LD",
                         waits=[("V", now("V"))])
                E("vector", lambda e, t=t: e.tensor_tensor(
                    S2[:, 0:D], SB1[:, 0:D], xres[:, t * D:(t + 1) * D], ALU.add),
                    waits=[("LD", ld)], drain=True, inc="V")
                ln_from(S2, S1, dst_fn(t), now("V"))

        if upto >= 4.05:
            barrier()
        if upto >= 4.1:
            for t in range(NT):
                quant_tile(arena[:, t * D:(t + 1) * D], SB1, waits0=[("V", now("V"))])
                transpose_region(
                    SB1, ICN,
                    (lambda t: (lambda ic: arena[:, t * D + 128 * ic: t * D + 128 * (ic + 1)]))(t),
                    now("V"))

        if upto >= 4.2:
            def wo_rhs(oc, icg):
                return wq_g[4 * oc + 3, 512 * icg:512 * (icg + 1), :].rearrange(
                    "(i p) o -> p i o", p=128)

            def lhsT_tmaj(ic, t):
                return arena[:, t * D + 128 * ic: t * D + 128 * (ic + 1)]

            proj_ocouter(wo_rhs, lhsT_tmaj, proj_ln_epilogue, rhs_cast=False)

        if upto >= 5:
            reload_res_ln(lambda t: xres[:, t * D:(t + 1) * D])

        if upto >= 6:
            barrier()
            gp_w = dma("gpsimd", smallb[:, 0:768].rearrange("p (i o) -> p i o", o=24),
                       wsm.rearrange("(i p) o -> p i o", p=128), "GP")
            gp_w = dma("gpsimd", arena[:, 0:8192].rearrange("p (i o) -> p i o", o=1024),
                       wbr.rearrange("(i p) o -> p i o", p=128), "GP")
            E("vector", lambda e: e.memset(S2[0:1, 0:128], 1.0), drain=True, inc="V")
            E("tensor", lambda e: e.matmul(pbB[:, 0:25], S2[0:1, 0:128],
                                           skB[0:1, 192:217], start=True, stop=True),
              waits=[("V", now("V"))], inc="T")
            E("vector", lambda e: e.tensor_copy(sm2[:, 96:121], pbB[:, 0:25]),
              waits=[("T", now("T"))], drain=True, inc="V")
            E("scalar", lambda e: e.activation(sm2[:, 121:122], sm2[:, 120:121], ACTF.Sigmoid),
              waits=[("V", now("V"))], inc="S")
            E("vector", lambda e: e.tensor_scalar(sm2[:, 122:123], sm2[:, 121:122],
                                                  -1.0, 1.0, ALU.mult, ALU.add),
              waits=[("S", now("S"))], drain=True)
            E("vector", lambda e: e.memset(sm2[:, 16:32], 0.0), drain=True)
            for z in (0, 5, 10, 15):
                E("vector", lambda e, z=z: e.memset(sm2[:, 16 + z:17 + z], 1.0), drain=True)
            E("vector", lambda e: e.tensor_scalar(
                sm2[:, 16:32], sm2[:, 16:32], sm2[:, 122:123], None, ALU.mult),
                drain=True, inc="V")
            W("tensor", [("V", now("V"))])

            for t in range(NT):
                E("scalar", lambda e, t=t: e.activation(
                    S2[:, 0:2048], xres[:, t * D:t * D + 2048], ACTF.Square,
                    accum_out=small[:, 8:9]), waits=[("V", now("V"))], inc="S")
                E("scalar", lambda e, t=t: e.activation(
                    S2[:, 2048:4096], xres[:, t * D + 2048:t * D + 4096], ACTF.Square,
                    accum_out=small[:, 9:10]), inc="S")
                E("vector", lambda e: e.tensor_tensor(
                    small[:, 10:11], small[:, 8:9], small[:, 9:10], ALU.add),
                    waits=[("S", now("S"))], drain=True)
                E("vector", lambda e: e.tensor_scalar(
                    small[:, 11:12], small[:, 10:11], 1.0 / D, RMS_EPS, ALU.mult, ALU.add),
                    drain=True, inc="V")
                E("scalar", lambda e: e.activation(
                    small[:, 12:13], small[:, 11:12], ACTF.Sqrt),
                    waits=[("V", now("V"))], inc="S")
                E("vector", lambda e: e.reciprocal(small[:, 13:14], small[:, 12:13]),
                  waits=[("S", now("S"))], drain=True)
                E("vector", lambda e, t=t: e.tensor_scalar(
                    SB1[:, 0:D], xres[:, t * D:(t + 1) * D], small[:, 13:14], None, ALU.mult),
                    drain=True, inc="V")
                transpose_region(SB1, ICN, lambda ic: wrhs[:, 128 * ic:128 * (ic + 1)],
                                 now("V"))
                for ic in range(ICN):
                    E("tensor", lambda e, ic=ic: e.matmul(
                        pbB[:, 0:24], wrhs[:, 128 * ic:128 * (ic + 1)],
                        smallb[:, 24 * ic:24 * (ic + 1)],
                        start=(ic == 0), stop=(ic == ICN - 1)),
                        waits=([("V", now("V")), ("GP", gp_w)] if ic == 0 else ()),
                        inc=("T" if ic == ICN - 1 else None))
                E("vector", lambda e: e.tensor_tensor(
                    S2[:, 0:24], pbB[:, 0:24], sm2[:, 96:120], ALU.add),
                    waits=[("T", now("T"))], drain=True)
                E("vector", lambda e, t=t: e.tensor_scalar(
                    skA[:, 16 * t:16 * (t + 1)], S2[:, 0:16], 10.0, None, ALU.mult), drain=True)
                E("vector", lambda e, t=t: e.tensor_copy(
                    skB[:, 128 + 8 * t:128 + 8 * t + 8], S2[:, 16:24]), drain=True, inc="V")
                W("tensor", [("V", now("V"))])

            zv = lambda ap: ap.rearrange("p (t z) -> p t z", z=16)
            mv = lambda ap: ap.rearrange("p (t i j) -> p t i j", i=4, j=4)
            mvT = lambda ap: ap.rearrange("p (t j i) -> p t i j", j=4, i=4)
            mvt = lambda ap: ap.rearrange("p (t j i) -> p t j i", j=4, i=4)
            ub = lambda: skB[:, 0:32].rearrange("p (t i) -> p t i", i=4).unsqueeze(3).broadcast_to((128, 8, 4, 4))
            vb = lambda: skB[:, 32:64].rearrange("p (t j) -> p t j", j=4).unsqueeze(2).broadcast_to((128, 8, 4, 4))
            E("vector", lambda e: e.memset(skB[:, 32:64], 0.0), drain=True)
            for it in range(SINKHORN_ITERS):
                E("vector", lambda e: e.tensor_tensor(
                    mv(S2[:, 0:128]), mv(skA[:, :]), vb(), ALU.add), drain=True)
                E("vector", lambda e: e.tensor_reduce(
                    skB[:, 64:96], S2[:, 0:128].rearrange("p (g j) -> p g j", j=4),
                    AX, ALU.max), drain=True)
                E("vector", lambda e: e.tensor_tensor(
                    mv(S2[:, 128:256]), mv(S2[:, 0:128]),
                    skB[:, 64:96].rearrange("p (t i) -> p t i", i=4).unsqueeze(3).broadcast_to((128, 8, 4, 4)),
                    ALU.subtract), drain=True, inc="V")
                E("scalar", lambda e: e.activation(S2[:, 0:128], S2[:, 128:256], ACTF.Exp),
                  waits=[("V", now("V"))], inc="S")
                E("vector", lambda e: e.tensor_reduce(
                    skB[:, 96:128], S2[:, 0:128].rearrange("p (g j) -> p g j", j=4),
                    AX, ALU.add), waits=[("S", now("S"))], drain=True, inc="V")
                E("scalar", lambda e: e.activation(skB[:, 192:224], skB[:, 96:128], ACTF.Ln),
                  waits=[("V", now("V"))], inc="S")
                E("vector", lambda e: e.scalar_tensor_tensor(
                    skB[:, 0:32], skB[:, 192:224], -1.0, skB[:, 64:96],
                    ALU.mult, ALU.subtract), waits=[("S", now("S"))], drain=True)
                E("vector", lambda e: e.tensor_tensor(
                    mvT(S2[:, 0:128]), mv(skA[:, :]), ub(), ALU.add), drain=True)
                E("vector", lambda e: e.tensor_reduce(
                    skB[:, 64:96], S2[:, 0:128].rearrange("p (g i) -> p g i", i=4),
                    AX, ALU.max), drain=True)
                E("vector", lambda e: e.tensor_tensor(
                    mvt(S2[:, 128:256]), mvt(S2[:, 0:128]),
                    skB[:, 64:96].rearrange("p (t j) -> p t j", j=4).unsqueeze(3).broadcast_to((128, 8, 4, 4)),
                    ALU.subtract), drain=True, inc="V")
                E("scalar", lambda e: e.activation(S2[:, 0:128], S2[:, 128:256], ACTF.Exp),
                  waits=[("V", now("V"))], inc="S")
                E("vector", lambda e: e.tensor_reduce(
                    skB[:, 96:128], S2[:, 0:128].rearrange("p (g i) -> p g i", i=4),
                    AX, ALU.add), waits=[("S", now("S"))], drain=True, inc="V")
                E("scalar", lambda e: e.activation(skB[:, 192:224], skB[:, 96:128], ACTF.Ln),
                  waits=[("V", now("V"))], inc="S")
                E("vector", lambda e: e.scalar_tensor_tensor(
                    skB[:, 32:64], skB[:, 192:224], -1.0, skB[:, 64:96],
                    ALU.mult, ALU.subtract), waits=[("S", now("S"))], drain=True)
            E("vector", lambda e: e.tensor_tensor(
                mv(S2[:, 0:128]), mv(skA[:, :]), ub(), ALU.add), drain=True)
            E("vector", lambda e: e.tensor_tensor(
                mv(S2[:, 128:256]), mv(S2[:, 0:128]), vb(), ALU.add),
                drain=True, inc="V")
            E("scalar", lambda e: e.activation(S2[:, 0:128], S2[:, 128:256], ACTF.Exp),
              waits=[("V", now("V"))], inc="S")
            E("vector", lambda e: e.scalar_tensor_tensor(
                zv(skA[:, :]), zv(S2[:, 0:128]), sm2[:, 121:122],
                sm2[:, 16:32].unsqueeze(1).broadcast_to((128, 8, 16)),
                ALU.mult, ALU.add), waits=[("S", now("S"))], drain=True, inc="V")

            for t in range(NT):
                for (o0, sidx) in ((0, 0), (4, 4)):
                    E("vector", lambda e, t=t, o0=o0, sidx=sidx: e.tensor_reduce(
                        small[:, sidx:sidx + 1], skB[:, 128 + 8 * t + o0:128 + 8 * t + o0 + 4],
                        AX, ALU.max), drain=True)
                    E("vector", lambda e, sidx=sidx: e.tensor_scalar(
                        small[:, sidx + 1:sidx + 2], small[:, sidx:sidx + 1], -1.0, None,
                        ALU.mult), drain=True, inc="V")
                    E("scalar", lambda e, t=t, o0=o0, sidx=sidx: e.activation(
                        sm2[:, 48 + o0:52 + o0], skB[:, 128 + 8 * t + o0:128 + 8 * t + o0 + 4],
                        ACTF.Exp, bias=small[:, sidx + 1:sidx + 2], scale=1.0,
                        accum_out=small[:, sidx + 2:sidx + 3]),
                        waits=[("V", now("V"))], inc="S")
                    E("vector", lambda e, sidx=sidx: e.reciprocal(
                        small[:, sidx + 3:sidx + 4], small[:, sidx + 2:sidx + 3]),
                        waits=[("S", now("S"))], drain=True)
                    E("vector", lambda e, o0=o0, sidx=sidx: e.tensor_scalar(
                        sm2[:, o0:o0 + 4], sm2[:, 48 + o0:52 + o0],
                        small[:, sidx + 3:sidx + 4], None, ALU.mult), drain=True, inc="V")
                E("vector", lambda e, t=t: e.tensor_scalar(
                    S2[:, 1024:2048], xres[:, t * D:t * D + 1024], sm2[:, 0:1], None,
                    ALU.mult), drain=True)
                for n in range(1, 4):
                    a, bgn = (1024, 2048) if n % 2 == 1 else (2048, 1024)
                    E("vector", lambda e, t=t, n=n, a=a, bgn=bgn: e.scalar_tensor_tensor(
                        S2[:, bgn:bgn + 1024], xres[:, t * D + 1024 * n:t * D + 1024 * (n + 1)],
                        sm2[:, n:n + 1], S2[:, a:a + 1024], ALU.mult, ALU.add), drain=True)
                E("vector", lambda e: e.tensor_copy(SB2[:, 0:1024], S2[:, 2048:3072]),
                  drain=True, inc="V")
                transpose_region(SB2[:, 0:1024], 8, lambda i: SB1[:, 128 * i:128 * (i + 1)],
                                 now("V"))
                for ic in range(8):
                    for oh in range(2):
                        E("tensor", lambda e, ic=ic, oh=oh: e.matmul(
                            pbB[:, 512 * oh:512 * (oh + 1)], SB1[:, 128 * ic:128 * (ic + 1)],
                            arena[:, 1024 * ic + 512 * oh:1024 * ic + 512 * (oh + 1)],
                            start=(ic == 0), stop=(ic == 7)),
                            waits=([("V", now("V")), ("GP", gp_w)] if (ic == 0 and oh == 0) else ()),
                            inc=("T" if (ic == 7 and oh == 1) else None))
                E("vector", lambda e: e.tensor_copy(S2[:, 1024:2048], pbB[:, 0:1024]),
                  waits=[("T", now("T"))], drain=True, inc="V")
                W("tensor", [("V", now("V"))])
                for n in range(4):
                    for k in range(4):
                        a, bgn = (2048, 3072) if k % 2 == 1 else (3072, 2048)
                        if k == 0:
                            E("vector", lambda e, t=t, n=n: e.tensor_scalar(
                                S2[:, 2048:3072], xres[:, t * D:t * D + 1024],
                                skA[:, 16 * t + 4 * n:16 * t + 4 * n + 1], None, ALU.mult),
                                drain=True)
                        else:
                            E("vector", lambda e, t=t, n=n, k=k, a=a, bgn=bgn: e.scalar_tensor_tensor(
                                S2[:, bgn:bgn + 1024], xres[:, t * D + 1024 * k:t * D + 1024 * (k + 1)],
                                skA[:, 16 * t + 4 * n + k:16 * t + 4 * n + k + 1],
                                S2[:, a:a + 1024], ALU.mult, ALU.add), drain=True)
                    E("vector", lambda e, n=n: e.scalar_tensor_tensor(
                        S1[:, 1024 * n:1024 * (n + 1)], S2[:, 1024:2048], sm2[:, 4 + n:5 + n],
                        S2[:, 3072:4096], ALU.mult, ALU.add),
                        drain=True, inc=("V" if n == 3 else None))
                E("vector", lambda e, t=t: e.tensor_tensor(
                    S2[:, 0:D], S1[:, 0:D], xres[:, t * D:(t + 1) * D], ALU.add),
                    drain=True, inc="V")
                ln_from(S2, S1, xres[:, t * D:(t + 1) * D], now("V"))

        if upto >= 7:
            barrier()
            for t in range(NT):
                transpose_region(
                    xres[:, t * D:(t + 1) * D], ICN,
                    (lambda t: (lambda ic: arena[:, ic * 1024 + t * 128: ic * 1024 + (t + 1) * 128]))(t),
                    now("V"))

            def wqkv_rhs(oc, icg):
                return wqkv[512 * icg:512 * (icg + 1), 512 * oc:512 * (oc + 1)].rearrange(
                    "(i p) o -> p i o", p=128)

            def xT_chunk(ic, t):
                return arena[:, ic * 1024 + t * 128: ic * 1024 + (t + 1) * 128]

            def qkv_st_epilogue(oc, t, tval):
                E("vector", lambda e, t=t: e.tensor_copy(SB2[:, 0:512], bank(t)),
                  waits=[("T", tval), ("LD", now("LD"))], drain=True, inc="V")
                dma("sync", qkvb[t, :, 512 * oc:512 * (oc + 1)], SB2[:, 0:512], "LD",
                    waits=[("V", now("V"))])

            proj_ocouter(wqkv_rhs, xT_chunk, qkv_st_epilogue, oc_count=12, rhs_cast=True)

        if upto >= 7.21:
            barrier()
        if upto >= 7.25:
            for t in range(NT):
                if t >= 4 and upto < 7.4:
                    continue
                if t < 4:
                    for kv in range(NCORES):
                        ld = dma("sync", wrhs[:, 128 * kv:128 * (kv + 1)],
                                 qkvb[t, :, 768 * kv + 640:768 * kv + 768], "LD",
                                 waits=[("T", now("T"))] if kv == 0 else ())
                    ldv = now("LD")
                    if upto < 7.28:
                        W("vector", [("LD", ldv)])
                        continue
                    for g0 in range(0, NCORES, 4):
                        for j in range(4):
                            kv = g0 + j
                            E("tensor", lambda e, kv=kv, j=j: e.transpose(
                                pbA[:, 0:512].bitcast(BF16)[:, 128 * j:128 * (j + 1)],
                                wrhs[:, 128 * kv:128 * (kv + 1)], ident[:, :]),
                                waits=([("LD", ldv)] if kv == 0 else ()), inc="T")
                        tt = now("T")
                        for j in range(4):
                            kv = g0 + j
                            for r in range(4):
                                E("vector", lambda e, kv=kv, j=j, r=r, t=t: e.tensor_copy(
                                    arena[:, t * D + 512 * kv + 128 * r: t * D + 512 * kv + 128 * (r + 1)],
                                    pbA[:, 0:512].bitcast(BF16)[:, 128 * j:128 * (j + 1)]),
                                    waits=([("T", tt)] if (j == 0 and r == 0) else ()),
                                    drain=(j == 0 and r == 0), inc="V")
                        W("tensor", [("V", now("V"))])
                else:
                    for kv in range(NCORES):
                        dma("sync", SB1[:, 512 * kv:512 * (kv + 1)],
                            qkvb[t, :, 768 * kv:768 * kv + 512], "LD",
                            waits=[("T", now("T"))] if kv == 0 else ())
                        dma("sync", SB2[:, 128 * kv:128 * (kv + 1)],
                            qkvb[t - 4, :, 768 * kv + 512:768 * kv + 640], "LD")
                        dma("sync", SB2[:, 1024 + 128 * kv:1024 + 128 * (kv + 1)],
                            qkvb[t, :, 768 * kv + 512:768 * kv + 640], "LD")
                        dma("sync", wrhs[:, 128 * kv:128 * (kv + 1)],
                            qkvb[t - 4, :, 768 * kv + 640:768 * kv + 768], "LD")
                        dma("sync", wrhs[:, 1024 + 128 * kv:1024 + 128 * (kv + 1)],
                            qkvb[t, :, 768 * kv + 640:768 * kv + 768], "LD")
                    ldall = now("LD")
                    E("vector", lambda e: e.tensor_tensor(
                        S2[:, 0:1024], SB2[:, 0:1024], SB2[:, 1024:2048], ALU.subtract),
                        waits=[("LD", ldall)], drain=True)
                    for kv in range(NCORES):
                        E("vector", lambda e, kv=kv: e.tensor_tensor(
                            S2[:, 1024:1536].rearrange("p (j d) -> p j d", d=128),
                            SB1[:, 512 * kv:512 * (kv + 1)].rearrange("p (j d) -> p j d", d=128),
                            S2[:, 128 * kv:128 * (kv + 1)].unsqueeze(1).broadcast_to((128, 4, 128)),
                            ALU.mult), drain=True)
                        E("vector", lambda e, kv=kv: e.tensor_reduce(
                            small[:, 4 * kv:4 * kv + 4],
                            S2[:, 1024:1536].rearrange("p (j d) -> p j d", d=128),
                            AX, ALU.add), drain=True)
                    E("vector", lambda e: e.memset(small[0:1, 63:64], 0.0), inc="V")
                    E("scalar", lambda e: e.activation(
                        small[:, 32:64], small[:, 0:32], ACTF.Sigmoid, scale=RSC),
                        waits=[("V", now("V"))], inc="S")
                    W("vector", [("S", now("S"))])
                    if upto < 7.5:
                        continue
                    for kv in range(NCORES):
                        E("vector", lambda e, kv=kv: e.tensor_tensor(
                            S2[:, 1024:1152], wrhs[:, 128 * kv:128 * (kv + 1)],
                            wrhs[:, 1024 + 128 * kv:1024 + 128 * (kv + 1)], ALU.subtract),
                            drain=True)
                        for r in range(4):
                            hh = 4 * kv + r
                            E("vector", lambda e, kv=kv, r=r, hh=hh: e.scalar_tensor_tensor(
                                SB2[:, 128 * r:128 * (r + 1)], S2[:, 1024:1152],
                                small[:, 32 + hh:33 + hh],
                                wrhs[:, 1024 + 128 * kv:1024 + 128 * (kv + 1)],
                                ALU.mult, ALU.add), drain=(r == 0), inc=("V" if r == 3 else None))
                        vv = now("V")
                        for r in range(4):
                            E("tensor", lambda e, r=r: e.transpose(
                                pbA[:, 0:512].bitcast(BF16)[:, 128 * r:128 * (r + 1)],
                                SB2[:, 128 * r:128 * (r + 1)], ident[:, :]),
                                waits=([("V", vv)] if r == 0 else ()), inc="T")
                        tt = now("T")
                        for r in range(4):
                            E("vector", lambda e, kv=kv, r=r, t=t: e.tensor_copy(
                                arena[:, t * D + 512 * kv + 128 * r: t * D + 512 * kv + 128 * (r + 1)],
                                pbA[:, 0:512].bitcast(BF16)[:, 128 * r:128 * (r + 1)]),
                                waits=([("T", tt)] if r == 0 else ()), drain=(r == 0), inc="V")
                        W("tensor", [("V", now("V"))])

        if upto >= 8:
            def wproj_rhs(oc, icg):
                return wproj[512 * icg:512 * (icg + 1), 512 * oc:512 * (oc + 1)].rearrange(
                    "(i p) o -> p i o", p=128)

            def lhsT_tmaj2(ic, t):
                return arena[:, t * D + 128 * ic: t * D + 128 * (ic + 1)]

            W("gpsimd", [("V", now("V"))])
            proj_ocouter(wproj_rhs, lhsT_tmaj2, proj_ln_epilogue, rhs_cast=True)
            reload_res_ln(lambda t: xres[:, t * D:(t + 1) * D])

        if upto >= 9:
            barrier()
            cst_cm.__exit__(None, None, None)
            cst_freed[0] = True
            wat = sbt("wat", [128, 2560], BF16)
            gp_wa = dma("gpsimd", wat[:, :].rearrange("p (i o) -> p i o", o=80),
                          wa.rearrange("(i p) o -> p i o", p=128), "GP")
            E("vector", lambda e: e.memset(smallb[0:1, 768:896], 1.0), drain=True, inc="V")
            for t in range(NT):
                transpose_region(
                    xres[:, t * D:(t + 1) * D], ICN,
                    (lambda t: (lambda ic: arena[:, ic * 1024 + t * 128: ic * 1024 + (t + 1) * 128]))(t),
                    now("V"))
                for ic in range(ICN):
                    E("tensor", lambda e, ic=ic, t=t: e.matmul(
                        pbB[:, 1024:1104], arena[:, ic * 1024 + t * 128: ic * 1024 + (t + 1) * 128],
                        wat[:, 80 * ic:80 * (ic + 1)],
                        start=(ic == 0), stop=(ic == ICN - 1)),
                        waits=([("V", now("V")), ("GP", gp_wa)] if ic == 0 else ()),
                        inc=("T" if ic == ICN - 1 else None))
                tval = now("T")
                E("vector", lambda e: e.tensor_copy(sm2[:, 0:16], pbB[:, 1024:1040]),
                  waits=[("T", tval)], drain=True)
                E("vector", lambda e: e.tensor_copy(S2[:, 0:64], pbB[:, 1040:1104]),
                  drain=True, inc="V")
                W("tensor", [("V", now("V"))])
                E("vector", lambda e: e.tensor_reduce(
                    small[:, 0:1], sm2[:, 0:16], AX, ALU.max), drain=True)
                E("vector", lambda e: e.tensor_scalar(
                    small[:, 1:2], small[:, 0:1], -1.0, None, ALU.mult), drain=True, inc="V")
                E("scalar", lambda e: e.activation(
                    sm2[:, 16:32], sm2[:, 0:16], ACTF.Exp, bias=small[:, 1:2], scale=1.0,
                    accum_out=small[:, 2:3]), waits=[("V", now("V"))], inc="S")
                E("vector", lambda e: e.reciprocal(small[:, 3:4], small[:, 2:3]),
                  waits=[("S", now("S"))], drain=True)
                E("vector", lambda e: e.tensor_scalar(
                    sm2[:, 0:16], sm2[:, 16:32], small[:, 3:4], None, ALU.mult), drain=True)
                E("vector", lambda e: e.tensor_reduce(
                    small[:, 4:5], sm2[:, 0:16], AX, ALU.max), drain=True)
                E("vector", lambda e: e.tensor_scalar(
                    sm2[:, 32:48], sm2[:, 0:16], small[:, 4:5], None, ALU.is_ge), drain=True)
                E("vector", lambda e: e.tensor_reduce(
                    small[:, 5:6], sm2[:, 32:48], AX, ALU.add), drain=True)
                E("vector", lambda e: e.tensor_scalar(
                    sm2[:, 48:64], sm2[:, 32:48], 1e30, None, ALU.mult), drain=True)
                E("vector", lambda e: e.tensor_tensor(
                    sm2[:, 32:48], sm2[:, 0:16], sm2[:, 48:64], ALU.subtract), drain=True)
                E("vector", lambda e: e.tensor_reduce(
                    small[:, 6:7], sm2[:, 32:48], AX, ALU.max), drain=True)
                E("vector", lambda e: e.tensor_scalar(
                    small[:, 7:8], small[:, 5:6], 1.5, None, ALU.is_gt), drain=True)
                E("vector", lambda e: e.tensor_tensor(
                    small[:, 8:9], small[:, 4:5], small[:, 6:7], ALU.subtract), drain=True)
                E("vector", lambda e: e.tensor_tensor(
                    small[:, 9:10], small[:, 7:8], small[:, 8:9], ALU.mult), drain=True)
                E("vector", lambda e: e.tensor_tensor(
                    small[:, 10:11], small[:, 6:7], small[:, 9:10], ALU.add), drain=True)
                E("vector", lambda e: e.tensor_tensor(
                    small[:, 11:12], small[:, 4:5], small[:, 10:11], ALU.add), drain=True)
                E("vector", lambda e: e.tensor_scalar(
                    small[:, 12:13], small[:, 11:12], 1e-6, None, ALU.add), drain=True)
                E("vector", lambda e: e.reciprocal(small[:, 13:14], small[:, 12:13]), drain=True)
                E("vector", lambda e: e.tensor_scalar(
                    small[:, 14:15], small[:, 13:14], 0.5, None, ALU.mult), drain=True)
                E("vector", lambda e: e.tensor_scalar(
                    sm2[:, 32:48], sm2[:, 0:16], small[:, 10:11], None, ALU.is_ge), drain=True)
                E("vector", lambda e: e.scalar_tensor_tensor(
                    sm2[:, 48:64], sm2[:, 0:16], small[:, 14:15], sm2[:, 32:48],
                    ALU.mult, ALU.mult), drain=True)
                E("vector", lambda e: e.tensor_tensor(
                    SB2[:, 0:1024].rearrange("p (ee r) -> p ee r", r=64),
                    sm2[:, 48:64].unsqueeze(2).broadcast_to((128, 16, 64)),
                    S2[:, 0:64].unsqueeze(1).broadcast_to((128, 16, 64)), ALU.mult),
                    drain=True, inc="V")
                transpose_region(SB2[:, 0:1024], 8, lambda i: SB1[:, 128 * i:128 * (i + 1)],
                                 now("V"))
                dma("sync", stg[t, :, :], SB1[:, 0:1024], "LD", waits=[("V", now("V"))])
                W("vector", [("LD", now("LD"))])

            def wbase_rhs(oc, icg):
                return wbase[512 * icg:512 * (icg + 1), 512 * oc:512 * (oc + 1)].rearrange(
                    "(i p) o -> p i o", p=128)

            def bias_mms(oc):
                g = dma("gpsimd", smallb[0:1, 0:512], bvec[0:1, 512 * oc:512 * (oc + 1)],
                        "GP", waits=[("T", now("T"))])
                for t in range(NT):
                    E("tensor", lambda e, t=t: e.matmul(
                        bank(t), smallb[0:1, 768:896], smallb[0:1, 0:512],
                        start=False, stop=True),
                        waits=([("GP", g)] if t == 0 else ()),
                        inc=("T" if t == NT - 1 else None))

            proj_ocouter(wbase_rhs, xT_chunk, proj_ln_epilogue, rhs_cast=True,
                         extra_mms=bias_mms)

            for t in range(NT):
                dma("sync", arena[:, 1024 * t:1024 * (t + 1)], stg[t, :, :], "LD",
                    waits=([("T", now("T"))] if t == 0 else ()))
            W("tensor", [("LD", now("LD"))])

            def bmat_rhs(oc, icg):
                return bmat[512 * icg:512 * (icg + 1), 512 * oc:512 * (oc + 1)].rearrange(
                    "(i p) o -> p i o", p=128)

            def terT_chunk(ic, t):
                return arena[:, 1024 * t + 128 * ic: 1024 * t + 128 * (ic + 1)]

            kvv = kv_c.rearrange("a (t p o) -> a t p o", t=NT, p=128)

            def delta_epilogue2(oc, t, tval):
                E("vector", lambda e, t=t: e.tensor_copy(SB2[:, 512:1024], bank(t)),
                  waits=[("T", tval), ("LD", now("LD"))], drain=True, inc="V")
                dma("sync", kvv[0, t, :, 512 * oc:512 * (oc + 1)], SB2[:, 512:1024],
                    "LD", waits=[("V", now("V"))])

            proj_ocouter(bmat_rhs, terT_chunk, delta_epilogue2, ic_count=8,
                         rhs_cast=True)

            for t in range(NT):
                ld1 = dma("sync", SB1[:, 0:D], qkvb[t, :, 0:D], "LD",
                          waits=[("V", now("V"))])
                ld2 = dma("sync", SB2[:, 0:2048], kvv[0, t, :, 0:2048], "LD")
                ld3 = dma("sync", wrhs[:, 0:2048], kvv[0, t, :, 2048:4096], "LD")
                E("vector", lambda e: e.tensor_tensor(
                    S2[:, 0:2048], SB1[:, 0:2048], SB2[:, 0:2048], ALU.add),
                    waits=[("LD", now("LD"))], drain=True)
                E("vector", lambda e: e.tensor_tensor(
                    S2[:, 2048:4096], SB1[:, 2048:4096], wrhs[:, 0:2048], ALU.add),
                    drain=True)
                E("vector", lambda e, t=t: e.tensor_tensor(
                    S1[:, 0:D], S2[:, 0:D], xres[:, t * D:(t + 1) * D], ALU.add),
                    drain=True, inc="V")
                ln_from(S1, S2, None, now("V"))
                E("vector", lambda e: e.tensor_scalar(
                    S2[:, 0:D], S1[:, 0:D], small[:, 48:49], small[:, 50:51],
                    ALU.mult, ALU.add), drain=True, inc="V")
                dma("sync", out[128 * t:128 * (t + 1), :], S2[:, 0:D], "LD",
                    waits=[("V", now("V"))])
            W("sync", [("LD", now("LD"))])

        if dbgout is not None:
            src = xres if ("dump_xres" in dbg) else arena
            for t in range(NT):
                E("vector", lambda e, t=t: e.tensor_copy(S1[:, 0:D], src[:, t * D:(t + 1) * D]),
                  waits=[("LD", now("LD")), ("S", now("S")), ("T", now("T"))],
                  drain=True, inc="V")
                dma("sync", dbgout[t, :, :], S1[:, 0:D], "LD", waits=[("V", now("V"))])
                W("vector", [("LD", now("LD"))])
            W("sync", [("LD", now("LD"))])

        if upto < 9:
            W("sync", [("LD", now("LD"))])
            if now("GP"):
                W("gpsimd", [("GP", now("GP"))])

        for eng in MK.ENGS:
            sec = getattr(block, eng)
            stream = mk.streams[eng]
            sec(lambda e, stream=stream: [fn(e) for fn in stream])
        if not cst_freed[0]:
            cst_cm.__exit__(None, None, None)
    return nc


def _host_inputs(inputs):
    f32 = np.float32
    bf16 = ml_dtypes.bfloat16
    x = np.asarray(inputs["L0_attn_hidden_states"], f32)
    w4_all = np.stack([np.asarray(inputs["L0_attn_W" + k], f32) for k in "qkvo"])
    wsm = np.concatenate([np.asarray(inputs["L0_ffn_W_res"], f32).T,
                          np.asarray(inputs["L0_ffn_W_pre"], f32).T,
                          np.asarray(inputs["L0_ffn_W_post"], f32).T], axis=1)
    hb = np.concatenate([np.asarray(inputs["L0_ffn_H_res_base"], f32).reshape(-1),
                         np.asarray(inputs["L0_ffn_H_pre_base"], f32).reshape(-1),
                         np.asarray(inputs["L0_ffn_H_post_base"], f32).reshape(-1),
                         np.asarray(inputs["L0_ffn_gate"], f32).reshape(-1)])[None, :]
    wbr = np.ascontiguousarray(np.asarray(inputs["L0_ffn_W_branch"], f32))
    wqkv = np.ascontiguousarray(np.asarray(inputs["L1_attn_weight_qkv"], f32).T)
    wproj = np.ascontiguousarray(np.asarray(inputs["L1_attn_weight_proj"], f32).T)
    wbase = np.ascontiguousarray(np.asarray(inputs["L1_ffn_base_w"], f32).T)
    bvec = np.asarray(inputs["L1_ffn_base_b"], f32)[None, :]
    wa = np.concatenate([np.asarray(inputs["L1_ffn_router_w"], f32).T,
                         np.asarray(inputs["L1_ffn_A"], f32).T], axis=1)
    bmat = np.ascontiguousarray(
        np.asarray(inputs["L1_ffn_B"], f32).transpose(0, 2, 1).reshape(1024, 4096))
    identv = np.eye(128, dtype=bf16)

    xr = x.reshape(2, 4, 8, 128, 4096)
    inv = 1.0 / (10000.0 ** (np.arange(64, dtype=np.float64) / 64.0))
    in_maps = []
    for c in range(NCORES):
        xin = np.ascontiguousarray(xr[:, :, c]).reshape(TLOC, D)
        w4 = np.ascontiguousarray(w4_all[:, 512 * c:512 * (c + 1), :])
        pos = (1024 * np.arange(4)[:, None] + 128 * c
               + np.arange(128)[None, :]).reshape(-1).astype(np.float64)
        ang = pos[:, None] * inv[None, :]
        cs = np.concatenate([np.cos(ang), np.sin(ang)], axis=1).astype(f32)
        kcol = np.arange(1024)
        vis = (128 * (kcol // 128) + kcol % 128)[None, :] <= (128 * c + np.arange(128))[:, None]
        mask = np.where(vis, np.float64(0.0), np.float64(-1e30)).astype(bf16)
        in_maps.append({
            "xin": xin, "w4": w4, "cs": cs, "maskin": mask, "identin": identv,
            "wsm": wsm, "hbase": hb, "wbr": wbr, "wqkv": wqkv, "wproj": wproj,
            "wbase": wbase, "bvec": bvec, "wa": wa, "bmat": bmat,
        })
    return in_maps


def kernel(**inputs):
    in_maps = _host_inputs(inputs)
    nc = build(upto=9)
    res = run_bass_kernel_spmd(nc, in_maps, core_ids=list(range(NCORES)))
    y = np.empty((2, 4, 8, 128, 4096), np.float32)
    for c in range(NCORES):
        y[:, :, c] = res.results[c]["out"].reshape(2, 4, 128, 4096)
    return y.reshape(2, 4096, 4096)
